# version 01 (final)
# baseline (speedup 1.0000x reference)
import numpy as np
import concourse.bass as bass
import concourse.mybir as mybir
from concourse.bass_utils import run_bass_kernel_spmd
from concourse.bass_core_rust import IndirectOffsetOnAxis
from contextlib import ExitStack

F32 = mybir.dt.float32
BF16 = mybir.dt.bfloat16
I32 = mybir.dt.int32
U32 = mybir.dt.uint32
ALU = mybir.AluOpType
AX = mybir.AxisListType
ACT = mybir.ActivationFunctionType

NCORE = 8
T = 32768
TPC = T // NCORE
H = 1024
E = 8
LT = TPC // 128
GT = T // 128
C = 3072
CR = C + 1
NCT = C // 128
I_ATT = 2048
GP_OUT = 2 * I_ATT
GU_OUT = 2048
I_FFN = 1024
HCH = H // 128
ICH = I_ATT // 128
EPS_LN = 1e-5
ALPHA = 1.702
LIMIT = 7.0
QT = 4
TQ = LT // QT
BIG = 1e30
BIGIDX = 65536.0
MAGIC = 12582912.0



def _apname(x):
    try:
        return x.tensor.name
    except Exception:
        return None


class VG:

    def __init__(self, v):
        self._v = v
        self._last = set()

    def _emit(self, fn, outs, ins, *args, **kwargs):
        innames = {n for n in (_apname(a) for a in ins) if n}
        if innames & self._last:
            self._v.drain()
            self._last = set()
        r = fn(*args, **kwargs)
        self._last |= {n for n in (_apname(a) for a in outs) if n}
        return r

    def drain(self):
        self._last = set()
        return self._v.drain()

    def nop(self, *a, **k):
        return self._v.nop(*a, **k)

    def wait_ge(self, *a, **k):
        return self._v.wait_ge(*a, **k)

    def partition_id(self):
        return self._v.partition_id()

    def snap(self, *a, **k):
        return self._v.snap(*a, **k)

    def memset(self, ap, c):
        return self._emit(self._v.memset, [ap], [], ap, c)

    def tensor_copy(self, out, in_):
        return self._emit(self._v.tensor_copy, [out], [in_], out, in_)

    def tensor_tensor(self, out, a, b, op):
        return self._emit(self._v.tensor_tensor, [out], [a, b], out, a, b, op)

    def tensor_add(self, out, a, b):
        return self._emit(self._v.tensor_add, [out], [a, b], out, a, b)

    def tensor_sub(self, out, a, b):
        return self._emit(self._v.tensor_sub, [out], [a, b], out, a, b)

    def tensor_mul(self, out, a, b):
        return self._emit(self._v.tensor_mul, [out], [a, b], out, a, b)

    def tensor_scalar(self, out, in0, s1, s2, op0, op1=ALU.bypass,
                      accum_out=None):
        ins = [in0] + [s for s in (s1, s2) if s is not None
                       and not isinstance(s, (int, float))]
        outs = [out] + ([accum_out] if accum_out is not None else [])
        return self._emit(self._v.tensor_scalar, outs, ins, out, in0, s1, s2,
                          op0, op1, accum_out=accum_out)

    def scalar_tensor_tensor(self, out, in0, scalar, in1, op0, op1):
        ins = [in0, in1] + ([scalar] if not isinstance(scalar, (int, float))
                            else [])
        return self._emit(self._v.scalar_tensor_tensor, [out], ins, out, in0,
                          scalar, in1, op0, op1)

    def tensor_reduce(self, out, in_, ax, op, apply_absolute_value=None,
                      negate=None):
        kw = {}
        if apply_absolute_value is not None:
            kw["apply_absolute_value"] = apply_absolute_value
        if negate is not None:
            kw["negate"] = negate
        return self._emit(self._v.tensor_reduce, [out], [in_], out, in_, ax,
                          op, **kw)

    def reciprocal(self, out, in_):
        return self._emit(self._v.reciprocal, [out], [in_], out, in_)

    def tensor_tensor_scan(self, out, d0, d1, init, op0, op1):
        ins = [d0, d1] + ([init] if not isinstance(init, (int, float)) else [])
        return self._emit(self._v.tensor_tensor_scan, [out], ins, out, d0, d1,
                          init, op0, op1)


class SC:
    def __init__(self, handle):
        self.h = handle
        self.v = 0

    def inc(self, instr, k=1):
        instr.then_inc(self.h, k)
        self.v += k
        return instr

    def dinc(self, instr):
        return self.inc(instr, 16)


def build_nc(debug=(), upto=5, p2stage=99):
    import os
    DIS = set(filter(None, os.environ.get("MKDIS", "").split(",")))
    nc = bass.Bass("TRN2", target_bir_lowering=False, detect_race_conditions=False)
    dbg = set(debug)

    x_d = (nc.dram_tensor("x", [T, H], F32, kind="ExternalInput").ap()
           if upto >= 3 else None)
    gnw_d = nc.dram_tensor("gnw", [1, H], F32, kind="ExternalInput").ap()
    ginw_d = nc.dram_tensor("ginw", [1, H], F32, kind="ExternalInput").ap()
    gatew_d = nc.dram_tensor("gatew", [E, H], F32, kind="ExternalInput").ap()
    gpn_d = nc.dram_tensor("gpn", [1, H], F32, kind="ExternalInput").ap()
    gpw_d = nc.dram_tensor("gpw", [GP_OUT, H], F32, kind="ExternalInput").ap()
    dpn_d = nc.dram_tensor("dpn", [1, I_ATT], F32, kind="ExternalInput").ap()
    dpw_d = nc.dram_tensor("dpw", [H, I_ATT], F32, kind="ExternalInput").ap()
    rw_d = (nc.dram_tensor("rw", [H, E], F32, kind="ExternalInput").ap()
            if upto >= 4 else None)
    wgu_d = (nc.dram_tensor("wgu", [E, H, GU_OUT], F32, kind="ExternalInput").ap()
             if upto >= 5 else None)
    gub_d = (nc.dram_tensor("gub", [1, E * GU_OUT], F32, kind="ExternalInput").ap()
             if upto >= 5 else None)
    wdn_d = (nc.dram_tensor("wdn", [E, I_FFN, H], F32, kind="ExternalInput").ap()
             if upto >= 5 else None)
    dnb_d = (nc.dram_tensor("dnb", [1, E * H], F32, kind="ExternalInput").ap()
             if upto >= 5 else None)
    xsh_d = nc.dram_tensor("xsh", [TPC, H], F32, kind="ExternalInput").ap()
    out_d = nc.dram_tensor("out", [TPC, H], F32, kind="ExternalOutput").ap()

    mask_sh_d = nc.dram_tensor("mask_sh", [128, LT * E], F32)
    mask_all_d = nc.dram_tensor("mask_all", [NCORE * 128, LT * E], F32)
    scal_d = nc.dram_tensor("scal_d", [1, 16], F32)
    cid_d = nc.dram_tensor("cid_d", [1, 1], I32)
    idw_loc_d = nc.dram_tensor("idw_loc", [E * CR, 2], F32)
    idw_all_d = nc.dram_tensor("idw_all", [E * CR, 2], F32)
    contrib_d = nc.dram_tensor("contrib", [CR, H], F32)
    contrib_all_d = nc.dram_tensor("contrib_all", [NCORE * CR, H], F32)
    posk_d = nc.dram_tensor("posk_d", [128, 2 * E * GT], F32)
    mid32_d = nc.dram_tensor("mid32", [TPC, H], F32)
    midfm_d = nc.dram_tensor("midfm", [LT, HCH, 128, 128], BF16)

    dbg_out = {}

    def dbgt(name, shape, dtype=F32):
        if name in dbg:
            dbg_out[name] = nc.dram_tensor("dbg_" + name, shape, dtype,
                                           kind="ExternalOutput").ap()
            return dbg_out[name]
        return None

    d_logits = dbgt("logits", [128, LT * E])
    d_mask = dbgt("mask", [NCORE * 128, LT * E])
    d_idw = dbgt("idw", [E * CR, 2])
    d_pos = dbgt("pos", [128, E * GT])
    d_xin = dbgt("xin", [C, H])
    d_contrib = dbgt("contrib", [C, H])
    d_mid = dbgt("mid", [TPC, H])
    d_wffn = dbgt("wffn", [128, LT * E])
    d_cidx = dbgt("cidx", [128, 2 * LT], U32)
    d_stats = dbgt("stats", [128, 8 * LT + 8])
    d_a2 = dbgt("a2", [128, 4 * LT])

    st0 = ExitStack()

    def mk_alloc(stack):
        def sbf(name, shape, dtype=F32):
            return stack.enter_context(nc.sbuf_tensor(name, shape, dtype))
        def psf(name, shape, dtype=F32):
            return stack.enter_context(nc.psum_tensor(name, shape, dtype))
        return sbf, psf

    sb0, ps0 = mk_alloc(st0)

    def sem(name):
        return SC(st0.enter_context(nc.semaphore(name)))

    lt_f = sb0("lt_f", [128, 128])
    ones_m = sb0("ones_m", [128, 128])
    iden_bf = sb0("iden_bf", [128, 128], BF16)
    iden_f = sb0("iden_f", [128, 128])
    ones_r = sb0("ones_r", [1, 128])
    ones_c = sb0("ones_c", [128, 1])
    negio8 = sb0("negio8", [128, E])
    cidtok = sb0("cidtok", [128, 1])
    i127gw = sb0("i127gw", [128, 1])
    i127gp = sb0("i127gp", [128, 1])
    i127dp = sb0("i127dp", [128, 1])
    traw_gw = sb0("traw_gw", [128, 1])
    traw_gp = sb0("traw_gp", [128, 1])
    traw_dp = sb0("traw_dp", [128, 1])
    mgw_b = sb0("mgw_b", [128, 1])
    mgp_b = sb0("mgp_b", [128, 1])
    mdp_b = sb0("mdp_b", [128, 1])
    sgw_b = sb0("sgw_b", [128, 1])
    sgp_b = sb0("sgp_b", [128, 1])
    sdp_b = sb0("sdp_b", [128, 1])
    zzr = sb0("zzr", [128, 512])
    junkb = sb0("junkb", [128, I_ATT])
    tot_sb = sb0("tot_sb", [1, 24])
    gpn_b = sb0("gpn_b", [128, H])
    dpn_b = sb0("dpn_b", [128, I_ATT])
    m1s = sb0("m1s", [128, LT])
    s2us = sb0("s2us", [128, LT])
    mxvs = sb0("mxvs", [128, LT])
    tcols = sb0("tcols", [128, LT])
    srts = sb0("srts", [128, LT])
    css = sb0("css", [128, LT])
    invls = sb0("invls", [128, LT])
    qeps = sb0("qeps", [128, LT])
    npms = sb0("npms", [128, LT])
    ssums = sb0("ssums", [128, LT])
    oh1s = sb0("oh1s", [128, LT * E])
    oh2s = sb0("oh2s", [128, LT * E])
    w1s = sb0("w1s", [128, LT])
    w2s = sb0("w2s", [128, LT])
    masks = sb0("masks", [128, LT * E])
    logit_s = sb0("logit_s", [128, LT * E])
    wqg_fm = sb0("wqg_fm", [128, HCH * E], BF16)
    trow1_u = sb0("trow1_u", [128, LT], U32)
    trow2_u = sb0("trow2_u", [128, LT], U32)
    cidx1_u = sb0("cidx1_u", [128, LT], U32)
    cidx2_u = sb0("cidx2_u", [128, LT], U32)
    payl1 = sb0("payl1", [128, 2 * LT])
    payl2 = sb0("payl2", [128, 2 * LT])
    idsw = sb0("idsw", [128, 2 * NCT])
    xidx_u = sb0("xidx_u", [128, NCT], U32)
    wffn_s = sb0("wffn_s", [128, LT * E])

    s_init = sem("s_init")
    s_const = sem("s_const")
    s_gt = sem("s_gt")
    s_gtv = sem("s_gtv")
    s_gte = sem("s_gte")
    s_wsc = sem("s_wsc")
    s_gatep = sem("s_gatep")
    s_xload = sem("s_xload")
    s_u = sem("s_u")
    s_g = sem("s_g")
    s_m = sem("s_m")
    s_sqrt = sem("s_sqrt")
    s_a1 = sem("s_a1")
    s_q = sem("s_q")
    s_tr = sem("s_tr")
    s_mmr = sem("s_mmr")
    s_post = sem("s_post")
    s_exp = sem("s_exp")
    s_msk = sem("s_msk")
    s_wlA = sem("s_wlA")
    s_wabs = sem("s_wabs")
    s_wred = sem("s_wred")
    s_gt2 = sem("s_gt2")
    s_wlB = sem("s_wlB")
    s_wtern = sem("s_wtern")
    s_wtr = sem("s_wtr")
    s_cc = sem("s_cc")
    s_ma = sem("s_ma")
    s_keep = sem("s_keep")
    s_csm = sem("s_csm")
    s_cse = sem("s_cse")
    s_cs = sem("s_cs")
    s_posm = sem("s_posm")
    s_pose = sem("s_pose")
    s_tab = sem("s_tab")
    s_pko = sem("s_pko")
    s_mskst = sem("s_mskst")
    s_scat = sem("s_scat")
    s_idw = sem("s_idw")
    s_idwc = sem("s_idwc")
    s_bar = sem("s_bar")
    s_dbgx = sem("s_dbgx")
    s_xg = sem("s_xg")
    s_ev = sem("s_ev")
    s_evz = sem("s_evz")
    s_esqx = sem("s_esqx")
    s_esqz = sem("s_esqz")
    s_et1 = sem("s_et1")
    s_et2 = sem("s_et2")
    s_esr1 = sem("s_esr1")
    s_esr2 = sem("s_esr2")
    s_eq1 = sem("s_eq1")
    s_etr1 = sem("s_etr1")
    s_egp = sem("s_egp")
    s_ezs = sem("s_ezs")
    s_ez = sem("s_ez")
    s_eq2 = sem("s_eq2")
    s_etr2 = sem("s_etr2")
    s_edp = sem("s_edp")
    s_ecv = sem("s_ecv")
    s_ecst = sem("s_ecst")
    s_cg = sem("s_cg")
    s_cx = sem("s_cx")
    s_cadd = sem("s_cadd")
    s_csq = sem("s_csq")
    s_cln = sem("s_cln")
    s_csr = sem("s_csr")
    s_cmid = sem("s_cmid")
    s_ctr = sem("s_ctr")
    s_cte = sem("s_cte")
    s_crt = sem("s_crt")
    s_csg = sem("s_csg")
    s_cdel = sem("s_cdel")
    s_cw = sem("s_cw")
    s_ctr2 = sem("s_ctr2")
    s_cmst = sem("s_cmst")
    s_rwl = sem("s_rwl")
    s_fpre = sem("s_fpre")
    s_fpv = sem("s_fpv")
    s_fw = sem("s_fw")
    s_fmf = sem("s_fmf")
    s_facz = sem("s_facz")
    s_fgu = sem("s_fgu")
    s_fgc = sem("s_fgc")
    s_fsl = sem("s_fsl")
    s_fac = sem("s_fac")
    s_fab = sem("s_fab")
    s_ftr = sem("s_ftr")
    s_fdn = sem("s_fdn")
    s_facc = sem("s_facc")
    s_fmid = sem("s_fmid")
    s_fsum = sem("s_fsum")
    s_fsq = sem("s_fsq")
    s_fsr = sem("s_fsr")
    s_foutv = sem("s_foutv")
    s_fout = sem("s_fout")

    NWU = 48
    NLOOP = max(LT + 3, NWU)

    st12 = ExitStack()
    sb1, ps1 = mk_alloc(st12)
    NXB = 3
    gg_b = sb1("gg_b", [128, H])
    gi_b = sb1("gi_b", [128, H])
    xts = [sb1(f"xt{i}", [128, H]) for i in range(NXB)]
    uts = [sb1(f"ut{i}", [128, H]) for i in range(2)]
    vts = [sb1(f"vt{i}", [128, H]) for i in range(3)]
    qbfs = [sb1(f"qbf{i}", [128, H], BF16) for i in range(3)]
    qfms = [sb1(f"qfm{i}", [128, H], BF16) for i in range(3)]
    qrnd = sb1("qrnd", [128, H])
    lp_s = sb1("lp_s", [128, E])
    lpp_s = sb1("lpp_s", [128, E])
    probs = [sb1(f"prob{i}", [128, E]) for i in range(2)]
    t8a = sb1("t8a", [128, E])
    sm1 = sb1("sm1", [128, 1])
    sm2 = sb1("sm2", [128, 1])
    srr = sb1("srr", [128, 1])
    se1 = sb1("se1", [128, 1])
    se2 = sb1("se2", [128, 1])
    se3 = sb1("se3", [128, 1])
    junkc = sb1("junkc", [128, 1])
    wstage = [sb1(f"wstage{i}", [128, H]) for i in range(2)]
    wabs = sb1("wabs", [128, 56])
    wred = sb1("wred", [128, 2])
    gate_sb = sb1("gate_sb", [E, H])
    gate_bf = sb1("gate_bf", [E, H], BF16)
    cidI = sb1("cidI", [128, 1], I32)
    keeps = sb1("keeps", [128, E * GT])
    poss = sb1("poss", [128, E * GT])
    cs_sb = sb1("cs_sb", [1, E * GT])
    cinc = sb1("cinc", [1, GT])
    ids_f = sb1("ids_f", [128, LT])
    acc32 = sb1("acc32", [128, LT])
    tmp32 = sb1("tmp32", [128, LT])
    e1v = sb1("e1v", [128, LT])
    e2v = sb1("e2v", [128, LT])
    k1v = sb1("k1v", [128, LT])
    k2v = sb1("k2v", [128, LT])
    p1v = sb1("p1v", [128, LT])
    p2v = sb1("p2v", [128, LT])
    f32a = sb1("f32a", [128, LT])
    a2dmp = sb1("a2dmp", [128, 4 * LT])
    posown = sb1("posown", [128, E * LT])
    keepown = sb1("keepown", [128, E * LT])
    ps_log = ps1("ps_log", [128, 1024])
    ps_sc = ps1("ps_sc", [128, 512])
    ps_gt8 = ps1("ps_gt8", [128, 2048], BF16)

    wtr_cum = [16 * HCH * k for k in range(NWU + 1)]
    gpw_fm = dpw_fm = None
    wbfs = None

    with nc.Block() as blk:

        @blk.gpsimd
        def _(g):
            pid = g.partition_id()
            g.memset(lt_f[:, :], 1.0)
            g.affine_select(out=lt_f[:, :], in_=lt_f[:, :], compare_op=ALU.is_gt,
                            fill=0.0, base=0, pattern=[[1, 128]],
                            channel_multiplier=-1)
            g.memset(iden_bf[:, :], 1.0)
            g.affine_select(out=iden_bf[:, :], in_=iden_bf[:, :],
                            compare_op=ALU.is_equal, fill=0.0, base=0,
                            pattern=[[1, 128]], channel_multiplier=-1)
            g.memset(iden_f[:, :], 1.0)
            g.affine_select(out=iden_f[:, :], in_=iden_f[:, :],
                            compare_op=ALU.is_equal, fill=0.0, base=0,
                            pattern=[[1, 128]], channel_multiplier=-1)
            g.memset(ones_r[:, :], 1.0)
            g.memset(ones_c[:, :], 1.0)
            g.memset(ones_m[:, :], 1.0)
            g.memset(zzr[:, :], 0.0)
            g.iota(negio8[:, :], [[-1, E]], base=0, channel_multiplier=0,
                   allow_small_or_imprecise_dtypes=True)
            g.iota(ids_f[:, :], [[128, LT]], base=0, channel_multiplier=1,
                   allow_small_or_imprecise_dtypes=True)
            with g.register("cidr") as r:
                g.reg_alu(r, pid, 1, ALU.mult)
                g.store(cid_d[0:1, 0:1], r)
            s_const.inc(g.nop())
            s_init.dinc(g.dma_start(out=cidI[:, :],
                                    in_=cid_d.ap().partition_broadcast(128)))
            s_init.dinc(g.dma_start(out=gg_b[:, :],
                                    in_=gnw_d.partition_broadcast(128)))
            s_init.dinc(g.dma_start(out=gi_b[:, :],
                                    in_=ginw_d.partition_broadcast(128)))
            s_init.dinc(g.dma_start(out=gpn_b[:, :],
                                    in_=gpn_d.partition_broadcast(128)))
            s_init.dinc(g.dma_start(out=dpn_b[:, :],
                                    in_=dpn_d.partition_broadcast(128)))
            init_done = s_init.v
            g.wait_ge(s_gtv.h, 1)
            s_gatep.dinc(g.dma_start(out=scal_d.ap()[0:1, 0:1],
                                     in_=tot_sb[0:1, 0:1]))
            g.wait_ge(s_gatep.h, 16)
            s_gatep.dinc(g.dma_start(
                out=traw_gw[:, :],
                in_=scal_d.ap()[0:1, 0:1].partition_broadcast(128)))
            g.wait_ge(s_init.h, init_done)
            for i in range(LT):
                g.wait_ge(s_xload.h, 16 * (i + 1))
                if i >= 2:
                    g.wait_ge(s_m.h, i - 1)
                if i >= 3:
                    g.wait_ge(s_q.h, i - 2)
                xb, ub, vb = xts[i % NXB], uts[i % 2], vts[i % 3]
                if "uv" in DIS:
                    s_u.inc(g.nop()); s_g.inc(g.nop())
                else:
                    s_u.inc(g.tensor_tensor(ub[:, :], xb[:, :], gg_b[:, :], ALU.mult))
                    s_g.inc(g.tensor_tensor(vb[:, :], ub[:, :], gi_b[:, :], ALU.mult))
            if p2stage < 1:
                s_bar.inc(g.nop()); return
            g.wait_ge(s_mskst.h, 16)
            s_cc.inc(g.collective_compute(
                "AllGather", ALU.bypass, replica_groups=[list(range(NCORE))],
                ins=[mask_sh_d.ap().opt()], outs=[mask_all_d.ap().opt()]))
            g.wait_ge(s_cc.h, 1)
            mar = mask_all_d.ap().rearrange("(r p) f -> p r f", p=128)
            s_ma.dinc(g.dma_start(out=xts[0][:, :], in_=mar[:, 0:4, :]))
            s_ma.dinc(g.dma_start(out=xts[1][:, :], in_=mar[:, 4:8, :]))
            s_ma.dinc(g.dma_start(
                out=idw_loc_d.ap()[0:24576, :].rearrange("(p b) c -> p (b c)", p=128),
                in_=zzr[:, 0:384]))
            s_ma.dinc(g.dma_start(
                out=idw_loc_d.ap()[24576:E * CR, :].rearrange("(o a) c -> o (a c)", o=1),
                in_=zzr[0:1, 0:16]))
            zero_done = s_ma.v
            s_ma.dinc(g.dma_start(out=contrib_d.ap()[C:CR, 0:512], in_=zzr[0:1, :]))
            s_ma.dinc(g.dma_start(out=contrib_d.ap()[C:CR, 512:1024], in_=zzr[0:1, :]))
            if p2stage < 2:
                s_bar.inc(g.nop()); return
            g.wait_ge(s_gtv.h, 2)
            s_gatep.dinc(g.dma_start(out=scal_d.ap()[0:1, 1:3],
                                     in_=tot_sb[0:1, 8:17:8],
                                     allow_slow_non_contiguous=True))
            g.wait_ge(s_gatep.h, 48)
            s_gatep.dinc(g.dma_start(
                out=traw_gp[:, :],
                in_=scal_d.ap()[0:1, 1:2].partition_broadcast(128)))
            s_gatep.dinc(g.dma_start(
                out=traw_dp[:, :],
                in_=scal_d.ap()[0:1, 2:3].partition_broadcast(128)))
            g.wait_ge(s_pose.h, E)
            s_pko.dinc(g.dma_start(out=posk_d.ap()[:, 0:E * GT], in_=poss[:, :]))
            s_pko.dinc(g.dma_start(out=posk_d.ap()[:, E * GT:], in_=keeps[:, :]))
            g.wait_ge(s_pko.h, 32)
            pkr = posk_d.ap().rearrange("p (h e g) -> p h e g", h=2, e=E)
            s_pko.dinc(g.dma_start(out=posown[:, :].rearrange("p (e g) -> p e g", e=E),
                                   in_=pkr[:, 0, :, :][:, :, bass.ts(pid, LT)]))
            s_pko.dinc(g.dma_start(out=keepown[:, :].rearrange("p (e g) -> p e g", e=E),
                                   in_=pkr[:, 1, :, :][:, :, bass.ts(pid, LT)]))
            if p2stage < 3:
                s_bar.inc(g.nop()); return
            g.wait_ge(s_tab.h, 2)
            g.wait_ge(s_ma.h, zero_done)
            bc_reg = g.to_reg(E * CR - 1)
            for gl in range(LT):
                s_scat.dinc(g.indirect_dma_start(
                    idw_loc_d.ap(), IndirectOffsetOnAxis(trow1_u[:, gl:gl + 1], 0),
                    payl1[:, 2 * gl:2 * gl + 2], None,
                    bounds_check=bc_reg, oob_is_err=False))
                s_scat.dinc(g.indirect_dma_start(
                    idw_loc_d.ap(), IndirectOffsetOnAxis(trow2_u[:, gl:gl + 1], 0),
                    payl2[:, 2 * gl:2 * gl + 2], None,
                    bounds_check=bc_reg, oob_is_err=False))
            g.wait_ge(s_scat.h, s_scat.v)
            if p2stage < 4:
                s_bar.inc(g.nop()); return
            s_cc.inc(g.collective_compute(
                "AllReduce", ALU.add, replica_groups=[list(range(NCORE))],
                ins=[idw_loc_d.ap().opt()], outs=[idw_all_d.ap().opt()]))
            g.wait_ge(s_cc.h, 2)
            s_idw.dinc(g.dma_start(
                out=idsw[:, :],
                in_=idw_all_d.ap()[bass.ts(pid, CR), :][0:C, :]
                .rearrange("(gg p) c -> p gg c", p=128)))
            if d_mask is not None:
                s_ma.dinc(g.dma_start(out=d_mask, in_=mask_all_d.ap()[:, :]))
            if d_idw is not None and p2stage >= 4:
                s_ma.dinc(g.dma_start(out=d_idw, in_=idw_all_d.ap()[:, :]))
            s_bar.inc(g.nop())

        @blk.sync
        def _(sy):
            for i in range(LT + 3):
                if i < LT:
                    if i >= NXB:
                        sy.wait_ge(s_u.h, i - NXB + 1)
                        sy.wait_ge(s_m.h, i - NXB + 1)
                    s_xload.dinc(sy.dma_start(
                        out=xts[i % NXB][:, :],
                        in_=xsh_d[128 * i:128 * (i + 1), :]))
                if i >= 3:
                    j = i - 3
                    sy.wait_ge(s_q.h, j + 1)
                    if j >= 3:
                        sy.wait_ge(s_mmr.h, j - 2)
                    qb, qf = qbfs[j % 3], qfms[j % 3]
                    for k in range(HCH):
                        if "tr" in DIS:
                            s_tr.dinc(sy.dma_start(
                                out=qf[:, 128 * k:128 * (k + 1)],
                                in_=qb[:, 128 * k:128 * (k + 1)]))
                        else:
                            s_tr.dinc(sy.dma_start(
                                out=qf[:, 128 * k:128 * (k + 1)],
                                in_=qb[:, 128 * k:128 * (k + 1)], transpose=True))
            sy.wait_ge(s_msk.h, LT)
            s_mskst.dinc(sy.dma_start(out=mask_sh_d.ap()[:, :], in_=masks[:, :]))
            if d_logits is not None:
                s_mskst.dinc(sy.dma_start(out=d_logits, in_=logit_s[:, :]))
            if d_a2 is not None:
                sy.wait_ge(s_msk.h, LT)
                s_mskst.dinc(sy.dma_start(out=d_a2, in_=a2dmp[:, :]))
            if d_stats is not None:
                sy.wait_ge(s_msk.h, LT)
                s_mskst.dinc(sy.dma_start(out=d_stats[0:1, 8 * LT + 4:8 * LT + 7]
                                          .rearrange("o f -> o f"),
                                          in_=tot_sb[0:1, 0:3]))
                s_mskst.dinc(sy.dma_start(out=d_stats[:, 8 * LT + 3:8 * LT + 4],
                                          in_=traw_gw[:, :],
                                          allow_slow_non_contiguous=True))
                s_mskst.dinc(sy.dma_start(out=d_stats[0:1, 8 * LT + 7:8 * LT + 8],
                                          in_=scal_d.ap()[0:1, 0:1]))
                for si, tens in enumerate([m1s, s2us, mxvs, tcols, srts, css,
                                           invls, qeps]):
                    s_mskst.dinc(sy.dma_start(
                        out=d_stats[:, LT * si:LT * (si + 1)], in_=tens[:, :]))
                s_mskst.dinc(sy.dma_start(out=d_stats[:, 8 * LT:8 * LT + 1],
                                          in_=i127gw[:, :],
                                          allow_slow_non_contiguous=True))
                s_mskst.dinc(sy.dma_start(out=d_stats[:, 8 * LT + 1:8 * LT + 2],
                                          in_=sgw_b[:, :],
                                          allow_slow_non_contiguous=True))
                s_mskst.dinc(sy.dma_start(out=d_stats[:, 8 * LT + 2:8 * LT + 3],
                                          in_=mgw_b[:, :],
                                          allow_slow_non_contiguous=True))
            if d_pos is not None and p2stage >= 1:
                sy.wait_ge(s_pose.h, E)
                s_mskst.dinc(sy.dma_start(out=d_pos, in_=poss[:, :]))
            if d_cidx is not None and p2stage >= 2:
                sy.wait_ge(s_tab.h, 2)
                s_mskst.dinc(sy.dma_start(out=d_cidx[:, 0:LT], in_=cidx1_u[:, :]))
                s_mskst.dinc(sy.dma_start(out=d_cidx[:, LT:2 * LT], in_=cidx2_u[:, :]))
            s_bar.inc(sy.nop())

        @blk.scalar
        def _(s):
            s_wlA.dinc(s.dma_start(out=gate_sb[:, :], in_=gatew_d))
            s.wait_ge(s_wlA.h, 16)
            s_wabs.inc(s.activation(junkb[0:E, 0:H], gate_sb[:, :], ACT.Abs,
                                    accum_out=wabs[0:E, 48:49]))

            def emit_exp(j):
                s.wait_ge(s_post.h, j + 1)
                if j >= 2:
                    s.wait_ge(s_msk.h, j - 1)
                if "exp" in DIS:
                    s_exp.inc(s.nop())
                else:
                    s_exp.inc(s.activation(probs[j % 2][:, :],
                                           logit_s[:, E * j:E * (j + 1)], ACT.Exp,
                                           bias=npms[:, j:j + 1],
                                           accum_out=ssums[:, j:j + 1]))

            wlA = [0] * NWU
            for i in range(NLOOP):
                if i < NWU and "wA" not in DIS:
                    if i < 32:
                        src = gpw_d[bass.ts(i, 128), :]
                    else:
                        ii, hh = (i - 32) // 2, (i - 32) % 2
                        src = dpw_d[bass.ts(ii, 128), 1024 * hh:1024 * (hh + 1)]
                    s_wlA.dinc(s.dma_start(out=wstage[i % 2][:, :], in_=src))
                    wlA[i] = s_wlA.v
                if i < LT:
                    s.wait_ge(s_xload.h, 16 * (i + 1))
                    s.wait_ge(s_u.h, i + 1)
                    if "sq" in DIS:
                        s_m.inc(s.nop())
                    else:
                        s.activation(junkb[:, 0:H], xts[i % NXB][:, :], ACT.Square,
                                     accum_out=m1s[:, i:i + 1])
                        s_m.inc(s.activation(junkb[:, 0:H], uts[i % 2][:, :],
                                             ACT.Square, accum_out=s2us[:, i:i + 1]))
                if i < NWU:
                    if "wA" in DIS:
                        s_wabs.inc(s.nop())
                    else:
                        s.wait_ge(s_wlA.h, wlA[i])
                        s_wabs.inc(s.activation(junkb[:, 0:H], wstage[i % 2][:, :],
                                                ACT.Abs, accum_out=wabs[:, i:i + 1]))
                if 1 <= i <= LT:
                    j = i - 1
                    s.wait_ge(s_a1.h, j + 1)
                    if "sqx" in DIS:
                        s_sqrt.inc(s.nop())
                    else:
                        s_sqrt.inc(s.activation(srts[:, j:j + 1], tcols[:, j:j + 1],
                                                ACT.Sqrt))
                if 3 <= i < LT + 3:
                    emit_exp(i - 3)
            s_bar.inc(s.nop())

        @blk.tensor
        def _(t):
            t.wait_ge(s_const.h, 1)
            t.wait_ge(s_wabs.h, 1)
            s_gt.inc(t.matmul(ps_sc[0:1, 0:1], wabs[0:E, 48:49], ones_c[0:E, 0:1],
                              start=True, stop=True))
            t.wait_ge(s_wtern.h, 1)
            for k in range(HCH):
                if k >= 2:
                    t.wait_ge(s_gte.h, k - 1)
                s_gt.inc(t.transpose(
                    ps_gt8[:, 1024 * (k % 2):1024 * (k % 2) + E],
                    gate_bf[:, 128 * k:128 * (k + 1)], iden_bf[0:E, 0:E]))
            t.wait_ge(s_gte.h, HCH)
            for j in range(LT):
                t.wait_ge(s_tr.h, 16 * HCH * (j + 1))
                if "serx" in DIS and j >= 1:
                    t.wait_ge(s_post.h, j)
                elif j >= 2:
                    t.wait_ge(s_post.h, j - 1)
                if "mm" not in DIS:
                    for k in range(HCH):
                        ins = t.matmul(ps_log[:, 512 * (j % 2):512 * (j % 2) + E],
                                       qfms[j % 3][:, 128 * k:128 * (k + 1)],
                                       wqg_fm[:, E * k:E * (k + 1)],
                                       start=(k == 0), stop=(k == HCH - 1))
                    s_mmr.inc(ins)
                else:
                    s_mmr.inc(t.nop())
            if p2stage < 1:
                s_bar.inc(t.nop()); return
            t.wait_ge(s_wred.h, 1)
            t.wait_ge(s_post.h, LT)
            t.matmul(ps_log[0:1, 0:1], wred[:, 0:1], ones_c[:, 0:1],
                     start=True, stop=True)
            s_gt2.inc(t.matmul(ps_log[0:1, 512:513], wred[:, 1:2], ones_c[:, 0:1],
                               start=True, stop=True))
            t.wait_ge(s_keep.h, 1)
            t.wait_ge(s_wsc.h, 2)
            for n in range(4):
                if n >= 2:
                    t.wait_ge(s_cse.h, n - 1)
                s_csm.inc(t.matmul(ps_log[0:1, 512 * (n % 2):512 * (n % 2 + 1)],
                                   ones_c[:, :], keeps[:, 512 * n:512 * (n + 1)],
                                   start=True, stop=True))
            t.wait_ge(s_cs.h, 1)
            t.wait_ge(s_cse.h, 4)
            for e in range(E):
                if e >= 2:
                    t.wait_ge(s_pose.h, e - 1)
                pp = ps_log[:, 512 * (e % 2):512 * (e % 2) + GT]
                t.matmul(pp, lt_f[:, :], keeps[:, GT * e:GT * (e + 1)],
                         start=True, stop=False)
                s_posm.inc(t.matmul(pp, ones_r[:, :],
                                    cs_sb[0:1, GT * e:GT * (e + 1)],
                                    start=False, stop=True))
            s_bar.inc(t.nop())

        @blk.vector
        def _(v):
            v = VG(v)
            pid = v.partition_id()
            v.wait_ge(s_const.h, 1)
            v.wait_ge(s_init.h, 16)
            v.tensor_copy(cidtok[:, :], cidI[:, :])
            v.tensor_scalar(cidtok[:, :], cidtok[:, :], float(TPC), None, ALU.mult)
            v.tensor_scalar(ids_f[:, :], ids_f[:, :], cidtok[:, :], None, ALU.add)
            v.wait_ge(s_gt.h, 1)
            s_gtv.inc(v.tensor_copy(tot_sb[0:1, 0:1], ps_sc[0:1, 0:1]))
            v.wait_ge(s_gatep.h, 32)
            v.tensor_scalar(se1[:, :], traw_gw[:, :], 1.0 / (E * H), None,
                            ALU.mult)
            v.tensor_scalar(mgw_b[:, :], se1[:, :], 1e-5, None, ALU.max)
            s_wsc.inc(v.nop())
            v.reciprocal(sgw_b[:, :], mgw_b[:, :])
            v.tensor_scalar(i127gw[:, :], mgw_b[:, :], 1.0 / 127, None, ALU.mult)
            v.tensor_scalar(qrnd[0:E, :], gate_sb[:, :], sgw_b[0:E, 0:1], MAGIC,
                            ALU.mult, ALU.add)
            v.tensor_scalar(qrnd[0:E, :], qrnd[0:E, :], -MAGIC, 1.0,
                            ALU.add, ALU.min)
            s_wtern.inc(v.tensor_scalar(gate_bf[:, :], qrnd[0:E, :], -1.0,
                                        None, ALU.max))
            for k in range(HCH):
                v.wait_ge(s_gt.h, 2 + k)
                s_gte.inc(v.tensor_copy(wqg_fm[:, E * k:E * (k + 1)],
                                        ps_gt8[:, 1024 * (k % 2):1024 * (k % 2) + E]))

            def stage_a1(i):
                v.wait_ge(s_m.h, i + 1)
                v.wait_ge(s_g.h, i + 1)
                if "a1" in DIS:
                    s_a1.inc(v.nop()); return
                v.tensor_scalar(se1[:, :], s2us[:, i:i + 1], 1.0 / H, None, ALU.mult)
                v.tensor_scalar(se2[:, :], m1s[:, i:i + 1], 1e-8 / H, 1e-14,
                                ALU.mult, ALU.add)
                v.tensor_add(tcols[:, i:i + 1], se1[:, :], se2[:, :])
                s_a1.inc(v.tensor_reduce(mxvs[:, i:i + 1], vts[i % 3][:, :], AX.X,
                                         ALU.max, apply_absolute_value=True))

            def stage_a2(i):
                v.wait_ge(s_sqrt.h, i + 1)
                if "a2" in DIS:
                    s_q.inc(v.nop()); return
                v.reciprocal(srr[:, :], srts[:, i:i + 1])
                v.tensor_mul(se1[:, :], mxvs[:, i:i + 1], srr[:, :])
                v.tensor_scalar(se2[:, :], se1[:, :], 1e-5, None, ALU.max)
                if d_a2 is not None:
                    v.tensor_copy(a2dmp[:, 4 * i:4 * i + 1], srr[:, :])
                    v.tensor_copy(a2dmp[:, 4 * i + 1:4 * i + 2], se1[:, :])
                    v.tensor_copy(a2dmp[:, 4 * i + 2:4 * i + 3], se2[:, :])
                    v.tensor_copy(a2dmp[:, 4 * i + 3:4 * i + 4], i127gw[:, :])
                v.tensor_mul(invls[:, i:i + 1], se2[:, :], i127gw[:, :])
                v.tensor_scalar(qeps[:, i:i + 1], invls[:, i:i + 1], 1.0 / 1024,
                                None, ALU.mult)
                v.reciprocal(se3[:, :], se2[:, :])
                v.tensor_mul(se3[:, :], se3[:, :], srr[:, :])
                v.tensor_scalar(css[:, i:i + 1], se3[:, :], 127.0, None, ALU.mult)
                v.tensor_scalar(qrnd[:, :], vts[i % 3][:, :], css[:, i:i + 1],
                                MAGIC, ALU.mult, ALU.add)
                v.tensor_scalar(qrnd[:, :], qrnd[:, :], -MAGIC, 127.0,
                                ALU.add, ALU.min)
                s_q.inc(v.tensor_scalar(qbfs[i % 3][:, :], qrnd[:, :], -128.0,
                                        None, ALU.max))

            def stage_b(j):
                v.wait_ge(s_mmr.h, j + 1)
                if "sb" in DIS:
                    s_post.inc(v.nop()); return
                lcol = logit_s[:, E * j:E * (j + 1)]
                if "sb1" not in DIS:
                    dst = lp_s[:, :] if "sb1b" in DIS else lcol
                    ssc = 1.0 if "sb1c" in DIS else invls[:, j:j + 1]
                    srcp = (t8a[:, :] if "sb1d" in DIS
                            else ps_log[:, 512 * (j % 2):512 * (j % 2) + E])
                    v.tensor_scalar(dst, srcp, ssc, None, ALU.mult)
                if "sb2" not in DIS:
                    v.tensor_reduce(npms[:, j:j + 1], lcol, AX.X, ALU.max,
                                    negate=True)
                if "sb3" not in DIS:
                    v.scalar_tensor_tensor(lp_s[:, :], negio8[:, :], qeps[:, j:j + 1],
                                           lcol, ALU.mult, ALU.add)
                if "sb4" not in DIS:
                    v.tensor_reduce(sm1[:, :], lp_s[:, :], AX.X, ALU.max)
                    v.tensor_scalar(oh1s[:, E * j:E * (j + 1)], lp_s[:, :], sm1[:, :],
                                    None, ALU.is_ge)
                if "sb5" not in DIS:
                    v.scalar_tensor_tensor(lpp_s[:, :], oh1s[:, E * j:E * (j + 1)],
                                           -BIG, lp_s[:, :], ALU.mult, ALU.add)
                    v.tensor_reduce(sm2[:, :], lpp_s[:, :], AX.X, ALU.max)
                    v.tensor_scalar(oh2s[:, E * j:E * (j + 1)], lpp_s[:, :],
                                    sm2[:, :], None, ALU.is_ge)
                s_post.inc(v.nop())

            def stage_c(j):
                v.wait_ge(s_exp.h, j + 1)
                if "sc" in DIS:
                    s_msk.inc(v.nop()); return
                pb = probs[j % 2]
                v.reciprocal(srr[:, :], ssums[:, j:j + 1])
                v.tensor_mul(t8a[:, :], pb[:, :], oh1s[:, E * j:E * (j + 1)])
                v.tensor_scalar(t8a[:, :], t8a[:, :], srr[:, :], None, ALU.mult,
                                ALU.add, accum_out=w1s[:, j:j + 1])
                v.tensor_mul(t8a[:, :], pb[:, :], oh2s[:, E * j:E * (j + 1)])
                v.tensor_scalar(t8a[:, :], t8a[:, :], srr[:, :], None, ALU.mult,
                                ALU.add, accum_out=w2s[:, j:j + 1])
                v.tensor_scalar(masks[:, E * j:E * (j + 1)],
                                oh1s[:, E * j:E * (j + 1)], w1s[:, j:j + 1], None,
                                ALU.mult)
                s_msk.inc(v.scalar_tensor_tensor(
                    masks[:, E * j:E * (j + 1)], oh2s[:, E * j:E * (j + 1)],
                    w2s[:, j:j + 1], masks[:, E * j:E * (j + 1)], ALU.mult, ALU.add))

            for i in range(LT + 4):
                if i < LT:
                    stage_a1(i)
                if 1 <= i <= LT:
                    stage_a2(i - 1)
                if 3 <= i < LT + 3:
                    stage_b(i - 3)
                if i >= 4:
                    stage_c(i - 4)

            if p2stage < 1:
                s_bar.inc(v.nop()); return
            v.wait_ge(s_wabs.h, NWU + 1)
            v.tensor_reduce(wred[:, 0:1], wabs[:, 0:32], AX.X, ALU.add)
            v.tensor_reduce(wred[:, 1:2], wabs[:, 32:48], AX.X, ALU.add)
            s_wred.inc(v.nop())
            v.wait_ge(s_gt2.h, 1)
            v.tensor_copy(tot_sb[0:1, 8:9], ps_log[0:1, 0:1])
            s_gtv.inc(v.tensor_copy(tot_sb[0:1, 16:17], ps_log[0:1, 512:513]))
            v.wait_ge(s_gatep.h, 80)
            v.tensor_scalar(se1[:, :], traw_gp[:, :], 1.0 / (GP_OUT * H),
                            None, ALU.mult)
            v.tensor_scalar(mgp_b[:, :], se1[:, :], 1e-5, None, ALU.max)
            v.tensor_scalar(se1[:, :], traw_dp[:, :], 1.0 / (H * I_ATT),
                            None, ALU.mult)
            v.tensor_scalar(mdp_b[:, :], se1[:, :], 1e-5, None, ALU.max)
            v.reciprocal(sgp_b[:, :], mgp_b[:, :])
            v.tensor_scalar(i127gp[:, :], mgp_b[:, :], 1.0 / 127, None, ALU.mult)
            v.reciprocal(sdp_b[:, :], mdp_b[:, :])
            v.tensor_scalar(i127dp[:, :], mdp_b[:, :], 1.0 / 127, None, ALU.mult)
            s_wsc.inc(v.nop())
            if p2stage < 1:
                s_bar.inc(v.nop()); return
            v.wait_ge(s_ma.h, 32)
            for e in range(E):
                v.tensor_scalar(keeps[:, GT * e:GT * e + 128],
                                xts[0][:, e::E], 0.0, None, ALU.is_gt)
                v.tensor_scalar(keeps[:, GT * e + 128:GT * (e + 1)],
                                xts[1][:, e::E], 0.0, None, ALU.is_gt)
            s_keep.inc(v.nop())
            for n in range(4):
                v.wait_ge(s_csm.h, n + 1)
                s_cse.inc(v.tensor_copy(cs_sb[0:1, 512 * n:512 * (n + 1)],
                                        ps_log[0:1, 512 * (n % 2):512 * (n % 2 + 1)]))
            for e in range(E):
                v.tensor_tensor_scan(cinc[0:1, :], cs_sb[0:1, GT * e:GT * (e + 1)],
                                     zzr[0:1, 0:GT], 0.0, ALU.add, ALU.add)
                v.tensor_sub(cs_sb[0:1, GT * e:GT * (e + 1)], cinc[0:1, :],
                             cs_sb[0:1, GT * e:GT * (e + 1)])
            s_cs.inc(v.nop())
            for e in range(E):
                v.wait_ge(s_posm.h, e + 1)
                pp = ps_log[:, 512 * (e % 2):512 * (e % 2) + GT]
                s_pose.inc(v.tensor_copy(poss[:, GT * e:GT * (e + 1)], pp))
            if p2stage < 2:
                s_bar.inc(v.nop()); return
            v.wait_ge(s_pko.h, 64)
            for ch, (ohs, ev_, kv_, pv_) in enumerate([
                    (oh1s, e1v, k1v, p1v), (oh2s, e2v, k2v, p2v)]):
                for e in range(E):
                    ohv = ohs[:, e::E]
                    po = posown[:, LT * e:LT * (e + 1)]
                    ke = keepown[:, LT * e:LT * (e + 1)]
                    v.scalar_tensor_tensor(f32a[:, :], po, float(C), ke,
                                           ALU.is_lt, ALU.mult)
                    if e == 0:
                        v.tensor_mul(pv_[:, :], po, ohv)
                        v.tensor_mul(kv_[:, :], f32a[:, :], ohv)
                        v.memset(ev_[:, :], 0.0)
                    else:
                        v.tensor_mul(tmp32[:, :], po, ohv)
                        v.tensor_add(pv_[:, :], pv_[:, :], tmp32[:, :])
                        v.tensor_mul(tmp32[:, :], f32a[:, :], ohv)
                        v.tensor_add(kv_[:, :], kv_[:, :], tmp32[:, :])
                        v.tensor_scalar(tmp32[:, :], ohv, float(e), None, ALU.mult)
                        v.tensor_add(ev_[:, :], ev_[:, :], tmp32[:, :])
            for ch, (ev_, kv_, pv_, trow, cidx, pl, wsv) in enumerate([
                    (e1v, k1v, p1v, trow1_u, cidx1_u, payl1, w1s),
                    (e2v, k2v, p2v, trow2_u, cidx2_u, payl2, w2s)]):
                v.tensor_scalar(acc32[:, :], ev_[:, :], float(CR), None, ALU.mult)
                v.tensor_add(acc32[:, :], acc32[:, :], pv_[:, :])
                v.tensor_scalar(tmp32[:, :], acc32[:, :], -BIGIDX, None, ALU.add)
                v.tensor_mul(tmp32[:, :], tmp32[:, :], kv_[:, :])
                v.tensor_scalar(f32a[:, :], tmp32[:, :], BIGIDX, None, ALU.add)
                v.tensor_copy(trow[:, :], f32a[:, :])
                v.tensor_scalar(tmp32[:, :], acc32[:, :], float(-C), None, ALU.add)
                v.tensor_mul(tmp32[:, :], tmp32[:, :], kv_[:, :])
                v.tensor_scalar(f32a[:, :], tmp32[:, :], float(C), None, ALU.add)
                v.tensor_copy(cidx[:, :], f32a[:, :])
                v.tensor_copy(pl[:, 0::2], ids_f[:, :])
                s_tab.inc(v.tensor_copy(pl[:, 1::2], wsv[:, :]))
            if p2stage < 4:
                s_bar.inc(v.nop()); return
            v.wait_ge(s_idw.h, 16)
            v.tensor_copy(xidx_u[:, :], idsw[:, 0::2])
            s_idwc.inc(v.drain())
            s_bar.inc(v.nop())

    st12.close()
    BAR_AB = 5
    st34 = ExitStack()
    sb3, ps3 = mk_alloc(st34)
    gpw_fm = sb3("gpw_fm", [128, HCH * GP_OUT], BF16)
    dpw_fm = sb3("dpw_fm", [128, ICH * H], BF16)
    wstageC = [sb3("wstageC0", [128, H])] * 2
    wbfs = [sb3("wbf0", [128, H], BF16)] * 2
    wiC = sb3("wiC", [128, H], I32)

    if upto >= 3:
        stC = ExitStack()
        sbc, psc = mk_alloc(stC)
        xins = [sbc(f"xin{i}", [128, H]) for i in range(2)]
        ves = [sbc(f"ve{i}", [128, H]) for i in range(2)]
        eqb1 = [sbc(f"eqb1_{i}", [128, H], BF16) for i in range(2)]
        eqf1 = [sbc("eqf1_0", [128, H], BF16)] * 2
        ernd = sbc("ernd", [128, I_ATT])
        silu_sb = sbc("silu_sb", [128, I_ATT])
        zs = [sbc("z0", [128, I_ATT])] * 2
        vzs = [sbc("vz0", [128, I_ATT])] * 2
        eqb2 = [sbc("eqb2_0", [128, I_ATT], BF16)] * 2
        eqf2 = [sbc("eqf2_0", [128, I_ATT], BF16)] * 2
        ctrb = [sbc("ctrb0", [128, H])] * 2
        em2x = sbc("em2x", [128, NCT])
        emxv = sbc("emxv", [128, NCT])
        ets = sbc("ets", [128, NCT])
        esr = sbc("esr", [128, NCT])
        einv = sbc("einv", [128, NCT])
        em2z = sbc("em2z", [128, NCT])
        emxz = sbc("emxz", [128, NCT])
        ets2 = sbc("ets2", [128, NCT])
        esr2 = sbc("esr2", [128, NCT])
        einv2 = sbc("einv2", [128, NCT])
        esrr = sbc("esrr", [128, 1])
        er1 = sbc("er1", [128, 1])
        er2 = sbc("er2", [128, 1])
        er3 = sbc("er3", [128, 1])
        ewsc = sbc("ewsc", [128, 1])
        ps_gp = psc("ps_gp", [128, 4096])

        with nc.Block() as blkC:

            @blkC.gpsimd
            def _(g):
                g.wait_ge(s_bar.h, BAR_AB)
                g.wait_ge(s_idwc.h, 1)
                for i in range(NCT + 2):
                    if i < NCT:
                        if i >= 2:
                            g.wait_ge(s_esqx.h, i - 1)
                            if d_xin is not None:
                                g.wait_ge(s_dbgx.h, 16 * (i - 1))
                        s_xg.dinc(g.indirect_dma_start(
                            xins[i % 2][:, :], None, x_d,
                            IndirectOffsetOnAxis(xidx_u[:, i:i + 1], 0)))
                    if 1 <= i < NCT + 1:
                        j = i - 1
                        g.wait_ge(s_xg.h, 16 * (j + 1))
                        if j >= 2:
                            g.wait_ge(s_eq1.h, j - 1)
                        s_ev.inc(g.tensor_tensor(ves[j % 2][:, :], xins[j % 2][:, :],
                                                 gpn_b[:, :], ALU.mult))
                    if i >= 2:
                        j = i - 2
                        g.wait_ge(s_ez.h, 4 * (j + 1))
                        if j >= 1:
                            g.wait_ge(s_eq2.h, j)
                        s_evz.inc(g.tensor_tensor(vzs[j % 2][:, :], zs[j % 2][:, :],
                                                  dpn_b[:, :], ALU.mult))
                s_bar.inc(g.nop())

            @blkC.scalar
            def _(s):
                s.wait_ge(s_bar.h, BAR_AB)
                for i in range(NCT + 3):
                    if i < NCT:
                        s.wait_ge(s_xg.h, 16 * (i + 1))
                        s_esqx.inc(s.activation(junkb[:, 0:H], xins[i % 2][:, :],
                                                ACT.Square,
                                                accum_out=em2x[:, i:i + 1]))
                    if 1 <= i <= NCT:
                        j = i - 1
                        s.wait_ge(s_et1.h, j + 1)
                        s_esr1.inc(s.activation(esr[:, j:j + 1], ets[:, j:j + 1],
                                                ACT.Sqrt))
                    if 2 <= i < NCT + 2:
                        j = i - 2
                        for c in range(4):
                            s.wait_ge(s_egp.h, 4 * j + c + 1)
                            s_ezs.inc(s.activation(
                                silu_sb[:, 512 * c:512 * (c + 1)],
                                ps_gp[:, 512 * c:512 * (c + 1)], ACT.Silu,
                                scale=einv[:, j:j + 1]))
                        s.wait_ge(s_ez.h, 4 * (j + 1))
                        s_esqz.inc(s.activation(junkb[:, :], zs[j % 2][:, :],
                                                ACT.Square,
                                                accum_out=em2z[:, j:j + 1]))
                        s.wait_ge(s_et2.h, j + 1)
                        s_esr2.inc(s.activation(esr2[:, j:j + 1], ets2[:, j:j + 1],
                                                ACT.Sqrt))
                s_bar.inc(s.nop())

            @blkC.vector
            def _(v):
                v = VG(v)
                v.wait_ge(s_bar.h, BAR_AB)
                for u in range(NWU):
                    sw = sgp_b if u < 32 else sdp_b
                    v.wait_ge(s_wlB.h, 16 * (u + 1))
                    if u >= 1:
                        v.wait_ge(s_wtr.h, int(wtr_cum[u]))
                    v.tensor_scalar(ernd[:, 0:H], wstageC[u % 2][:, :], sw[:, :],
                                    MAGIC, ALU.mult, ALU.add)
                    v.tensor_scalar(ernd[:, 0:H], ernd[:, 0:H], -MAGIC, 1.0,
                                    ALU.add, ALU.min)
                    s_wtern.inc(v.tensor_scalar(wbfs[u % 2][:, :], ernd[:, 0:H],
                                                -1.0, None, ALU.max))

                def q1(i):
                    v.wait_ge(s_esqx.h, i + 1)
                    v.wait_ge(s_ev.h, i + 1)
                    v.tensor_scalar(ets[:, i:i + 1], em2x[:, i:i + 1], 1.0 / H,
                                    1e-8, ALU.mult, ALU.add)
                    s_et1.inc(v.tensor_reduce(emxv[:, i:i + 1], ves[i % 2][:, :],
                                              AX.X, ALU.max,
                                              apply_absolute_value=True))
                    v.wait_ge(s_esr1.h, i + 1)
                    v.reciprocal(esrr[:, :], esr[:, i:i + 1])
                    v.tensor_mul(er1[:, :], emxv[:, i:i + 1], esrr[:, :])
                    v.tensor_scalar(er2[:, :], er1[:, :], 1e-5, None, ALU.max)
                    v.tensor_mul(einv[:, i:i + 1], er2[:, :], i127gp[:, :])
                    v.reciprocal(er3[:, :], er2[:, :])
                    v.tensor_mul(er3[:, :], er3[:, :], esrr[:, :])
                    v.tensor_scalar(er3[:, :], er3[:, :], 127.0, None, ALU.mult)
                    v.tensor_scalar(ernd[:, 0:H], ves[i % 2][:, :], er3[:, :],
                                    MAGIC, ALU.mult, ALU.add)
                    v.tensor_scalar(ernd[:, 0:H], ernd[:, 0:H], -MAGIC, 127.0,
                                    ALU.add, ALU.min)
                    if i >= 2:
                        v.wait_ge(s_etr1.h, 16 * HCH * (i - 1))
                    s_eq1.inc(v.tensor_scalar(eqb1[i % 2][:, :], ernd[:, 0:H],
                                              -128.0, None, ALU.max))

                def zchunks(j):
                    if j >= 1:
                        v.wait_ge(s_evz.h, j)
                        v.wait_ge(s_esqz.h, j)
                    for c in range(4):
                        v.wait_ge(s_ezs.h, 4 * j + c + 1)
                        s_ez.inc(v.scalar_tensor_tensor(
                            zs[j % 2][:, 512 * c:512 * (c + 1)],
                            silu_sb[:, 512 * c:512 * (c + 1)], einv[:, j:j + 1],
                            ps_gp[:, 2048 + 512 * c:2048 + 512 * (c + 1)],
                            ALU.mult, ALU.mult))

                def q2(j):
                    v.wait_ge(s_esqz.h, j + 1)
                    v.wait_ge(s_evz.h, j + 1)
                    v.tensor_scalar(ets2[:, j:j + 1], em2z[:, j:j + 1],
                                    1.0 / I_ATT, 1e-8, ALU.mult, ALU.add)
                    s_et2.inc(v.tensor_reduce(emxz[:, j:j + 1], vzs[j % 2][:, :],
                                              AX.X, ALU.max,
                                              apply_absolute_value=True))
                    v.wait_ge(s_esr2.h, j + 1)
                    v.reciprocal(esrr[:, :], esr2[:, j:j + 1])
                    v.tensor_mul(er1[:, :], emxz[:, j:j + 1], esrr[:, :])
                    v.tensor_scalar(er2[:, :], er1[:, :], 1e-5, None, ALU.max)
                    v.tensor_mul(einv2[:, j:j + 1], er2[:, :], i127dp[:, :])
                    v.reciprocal(er3[:, :], er2[:, :])
                    v.tensor_mul(er3[:, :], er3[:, :], esrr[:, :])
                    v.tensor_scalar(er3[:, :], er3[:, :], 127.0, None, ALU.mult)
                    v.tensor_scalar(ernd[:, :], vzs[j % 2][:, :], er3[:, :],
                                    MAGIC, ALU.mult, ALU.add)
                    v.tensor_scalar(ernd[:, :], ernd[:, :], -MAGIC, 127.0,
                                    ALU.add, ALU.min)
                    if j >= 1:
                        v.wait_ge(s_etr2.h, 16 * ICH * j)
                    s_eq2.inc(v.tensor_scalar(eqb2[j % 2][:, :], ernd[:, :],
                                              -128.0, None, ALU.max))

                def contrib(j):
                    v.wait_ge(s_edp.h, j + 1)
                    v.tensor_mul(ewsc[:, :], einv2[:, j:j + 1],
                                 idsw[:, 2 * j + 1:2 * j + 2])
                    if j >= 1:
                        v.wait_ge(s_ecst.h, 16 * j * (2 if d_contrib is not None else 1))
                    s_ecv.inc(v.tensor_scalar(ctrb[j % 2][:, :],
                                              ps_gp[:, 2048:2048 + H],
                                              ewsc[:, :], None, ALU.mult))

                for i in range(NCT + 3):
                    if 2 <= i < NCT + 2:
                        q2(i - 2)
                    if i >= 3:
                        contrib(i - 3)
                    if i < NCT:
                        q1(i)
                    if 1 <= i < NCT + 1:
                        zchunks(i - 1)
                s_bar.inc(v.nop())

            def _w_tr(sy, i):
                sy.wait_ge(s_wtern.h, i + 2)
                wb = wbfs[i % 2]
                if i < 32:
                    for k in range(HCH):
                        s_wtr.dinc(sy.dma_start(
                            out=gpw_fm[:, GP_OUT * k + 128 * i:
                                       GP_OUT * k + 128 * (i + 1)],
                            in_=wb[:, 128 * k:128 * (k + 1)], transpose=True))
                else:
                    ii, hh = (i - 32) // 2, (i - 32) % 2
                    for k in range(HCH):
                        j = 8 * hh + k
                        s_wtr.dinc(sy.dma_start(
                            out=dpw_fm[:, H * j + 128 * ii:H * j + 128 * (ii + 1)],
                            in_=wb[:, 128 * k:128 * (k + 1)], transpose=True))

            @blkC.sync
            def _(sy):
                sy.wait_ge(s_bar.h, BAR_AB)
                for u in range(NWU):
                    if u < 32:
                        srcw = gpw_d[bass.ts(u, 128), :]
                    else:
                        ii, hh = (u - 32) // 2, (u - 32) % 2
                        srcw = dpw_d[bass.ts(ii, 128), 1024 * hh:1024 * (hh + 1)]
                    if u >= 1:
                        sy.wait_ge(s_wtern.h, u + 1)
                    s_wlB.dinc(sy.dma_start(out=wstageC[u % 2][:, :], in_=srcw))
                    if u >= 1:
                        _w_tr(sy, u - 1)
                _w_tr(sy, NWU - 1)
                for i in range(NCT + 3):
                    if i < NCT and d_xin is not None:
                        sy.wait_ge(s_xg.h, 16 * (i + 1))
                        s_dbgx.dinc(sy.dma_start(
                            out=d_xin[bass.ts(i, 128), :],
                            in_=xins[i % 2][:, :]))
                    if 2 <= i < NCT + 2:
                        j = i - 2
                        sy.wait_ge(s_eq2.h, j + 1)
                        if j >= 1:
                            sy.wait_ge(s_edp.h, j)
                        for k in range(ICH):
                            s_etr2.dinc(sy.dma_start(
                                out=eqf2[j % 2][:, 128 * k:128 * (k + 1)],
                                in_=eqb2[j % 2][:, 128 * k:128 * (k + 1)],
                                transpose=True))
                    if i < NCT:
                        sy.wait_ge(s_eq1.h, i + 1)
                        if i >= 1:
                            sy.wait_ge(s_egp.h, 4 * i)
                        for k in range(HCH):
                            s_etr1.dinc(sy.dma_start(
                                out=eqf1[i % 2][:, 128 * k:128 * (k + 1)],
                                in_=eqb1[i % 2][:, 128 * k:128 * (k + 1)],
                                transpose=True))
                    if i >= 3:
                        j = i - 3
                        sy.wait_ge(s_ecv.h, j + 1)
                        s_ecst.dinc(sy.dma_start(
                            out=contrib_d.ap()[bass.ts(j, 128), :],
                            in_=ctrb[j % 2][:, :]))
                        if d_contrib is not None:
                            s_ecst.dinc(sy.dma_start(
                                out=d_contrib[bass.ts(j, 128), :],
                                in_=ctrb[j % 2][:, :]))

                s_bar.inc(sy.nop())

            @blkC.tensor
            def _(t):
                t.wait_ge(s_bar.h, BAR_AB)
                t.wait_ge(s_wtr.h, int(wtr_cum[NWU]))
                for i in range(NCT + 1):
                    if i < NCT:
                        t.wait_ge(s_etr1.h, 16 * HCH * (i + 1))
                        for c in range(4):
                            if i >= 1:
                                t.wait_ge(s_ezs.h, 4 * (i - 1) + c + 1)
                            for k in range(HCH):
                                t.matmul(ps_gp[:, 512 * c:512 * (c + 1)],
                                         eqf1[i % 2][:, 128 * k:128 * (k + 1)],
                                         gpw_fm[:, GP_OUT * k + 512 * c:
                                                GP_OUT * k + 512 * (c + 1)],
                                         start=(k == 0), stop=(k == HCH - 1))
                            if i >= 1:
                                t.wait_ge(s_ez.h, 4 * (i - 1) + c + 1)
                            if c <= 1 and i >= 1:
                                t.wait_ge(s_ecv.h, i - 1)
                            for k in range(HCH):
                                ins = t.matmul(
                                    ps_gp[:, 2048 + 512 * c:2048 + 512 * (c + 1)],
                                    eqf1[i % 2][:, 128 * k:128 * (k + 1)],
                                    gpw_fm[:, GP_OUT * k + 2048 + 512 * c:
                                           GP_OUT * k + 2048 + 512 * (c + 1)],
                                    start=(k == 0), stop=(k == HCH - 1))
                            s_egp.inc(ins)
                    if i >= 1:
                        j = i - 1
                        t.wait_ge(s_etr2.h, 16 * ICH * (j + 1))
                        t.wait_ge(s_ez.h, 4 * min(j + 2, NCT))
                        for n in range(2):
                            for k in range(ICH):
                                ins = t.matmul(
                                    ps_gp[:, 2048 + 512 * n:2048 + 512 * (n + 1)],
                                    eqf2[j % 2][:, 128 * k:128 * (k + 1)],
                                    dpw_fm[:, H * k + 512 * n:
                                           H * k + 512 * (n + 1)],
                                    start=(k == 0), stop=(k == ICH - 1))
                        s_edp.inc(ins)
                s_bar.inc(t.nop())

        stC.close()
    st34.close()
    BAR_C = BAR_AB + 5

    if upto >= 4:
        stD = ExitStack()
        sbd, psd = mk_alloc(stD)
        g1s = [sbd(f"g1_{i}", [128, H]) for i in range(2)]
        g2s = [sbd(f"g2_{i}", [128, H]) for i in range(2)]
        xrs = [sbd(f"xr{i}", [128, H]) for i in range(2)]
        sss = [sbd(f"ss{i}", [128, H]) for i in range(2)]
        mids = [sbd(f"mid{i}", [128, H]) for i in range(2)]
        midb = [sbd(f"midb{i}", [128, H], BF16) for i in range(2)]
        mfmb = [sbd(f"mfmb{i}", [128, H], BF16) for i in range(2)]
        fmf32 = sbd("fmf32", [128, H])
        rw_sb = sbd("rw_sb", [128, HCH * E])
        cmu = sbd("cmu", [128, LT])
        cnmu = sbd("cnmu", [128, LT])
        csq = sbd("csq", [128, LT])
        ctln = sbd("ctln", [128, LT])
        csrt = sbd("csrt", [128, LT])
        cdtmp = sbd("cdtmp", [128, 1])
        de1 = sbd("de1", [128, 1])
        de2 = sbd("de2", [128, 1])
        de3 = sbd("de3", [128, 1])
        dm1 = sbd("dm1", [128, 1])
        dm2 = sbd("dm2", [128, 1])
        cdel = sbd("cdel", [128, LT])
        cw2 = sbd("cw2", [128, LT])
        cw1 = sbd("cw1", [128, LT])
        lrt = sbd("lrt", [128, E])
        lrp = sbd("lrp", [128, E])
        lrpp = sbd("lrpp", [128, E])
        cmx = sbd("cmx", [128, 1])
        ceps = sbd("ceps", [128, 1])
        ps_ctr = psd("ps_ctr", [128, 1024])
        ps_rt = psd("ps_rt", [128, 512])

        with nc.Block() as blkD:

            @blkD.gpsimd
            def _(g):
                g.wait_ge(s_bar.h, BAR_C)
                g.wait_ge(s_ecst.h, s_ecst.v)
                s_cc.inc(g.collective_compute(
                    "AllGather", ALU.bypass, replica_groups=[list(range(NCORE))],
                    ins=[contrib_d.ap().opt()], outs=[contrib_all_d.ap().opt()]))
                cc_ag2 = s_cc.v
                g.wait_ge(s_cc.h, cc_ag2)
                for i in range(LT + 1):
                    if i < LT:
                        if i >= 2:
                            g.wait_ge(s_cadd.h, i - 1)
                        s_cg.dinc(g.indirect_dma_start(
                            g1s[i % 2][:, :], None, contrib_all_d.ap(),
                            IndirectOffsetOnAxis(cidx1_u[:, i:i + 1], 0)))
                        s_cg.dinc(g.indirect_dma_start(
                            g2s[i % 2][:, :], None, contrib_all_d.ap(),
                            IndirectOffsetOnAxis(cidx2_u[:, i:i + 1], 0)))
                    if i >= 1:
                        j = i - 1
                        g.wait_ge(s_cg.h, 32 * (j + 1))
                        g.wait_ge(s_cx.h, 16 * (j + 1))
                        if j >= 2:
                            g.wait_ge(s_cmid.h, j - 1)
                        g.tensor_tensor(sss[j % 2][:, :], xrs[j % 2][:, :],
                                        g1s[j % 2][:, :], ALU.add)
                        s_cadd.inc(g.tensor_tensor(sss[j % 2][:, :], sss[j % 2][:, :],
                                                   g2s[j % 2][:, :], ALU.add))
                s_bar.inc(g.nop())

            @blkD.sync
            def _(sy):
                sy.wait_ge(s_bar.h, BAR_C)
                for k in range(HCH):
                    s_rwl.dinc(sy.dma_start(out=rw_sb[:, E * k:E * (k + 1)],
                                            in_=rw_d[bass.ts(k, 128), :]))
                for i in range(LT + 2):
                    if i < LT:
                        if i >= 2:
                            sy.wait_ge(s_cadd.h, i - 1)
                        s_cx.dinc(sy.dma_start(
                            out=xrs[i % 2][:, :],
                            in_=xsh_d[128 * i:128 * (i + 1), :]))
                    if i >= 2:
                        j = i - 2
                        sy.wait_ge(s_cmid.h, j + 1)
                        s_cmst.dinc(sy.dma_start(
                            out=mid32_d.ap()[bass.ts(j, 128), :], in_=mids[j % 2][:, :]))
                        if d_mid is not None:
                            s_cmst.dinc(sy.dma_start(
                                out=d_mid[bass.ts(j, 128), :], in_=mids[j % 2][:, :]))
                        for k in range(HCH):
                            s_ctr2.dinc(sy.dma_start(
                                out=mfmb[j % 2][:, 128 * k:128 * (k + 1)],
                                in_=midb[j % 2][:, 128 * k:128 * (k + 1)],
                                transpose=True))
                        sy.wait_ge(s_ctr2.h, s_ctr2.v)
                        s_cmst.dinc(sy.dma_start(
                            out=midfm_d.ap()[bass.ts(j, 1), :, :, :]
                            .rearrange("o k p c2 -> (o p) k c2"),
                            in_=mfmb[j % 2][:, :]
                            .rearrange("p (k c2) -> p k c2", c2=128)))
                if d_wffn is not None:
                    sy.wait_ge(s_cw.h, LT)
                    s_cmst.dinc(sy.dma_start(out=d_wffn, in_=wffn_s[:, :]))
                s_bar.inc(sy.nop())

            @blkD.scalar
            def _(s):
                s.wait_ge(s_bar.h, BAR_C)
                for i in range(LT + 3):
                    if i < LT:
                        s.wait_ge(s_cadd.h, i + 1)
                        s_csq.inc(s.activation(junkb[:, 0:H], sss[i % 2][:, :],
                                               ACT.Square,
                                               accum_out=csq[:, i:i + 1]))
                    if 1 <= i <= LT:
                        j = i - 1
                        s.wait_ge(s_cln.h, j + 1)
                        s_csr.inc(s.activation(csrt[:, j:j + 1], ctln[:, j:j + 1],
                                               ACT.Sqrt))
                    if 2 <= i < LT + 2:
                        j = i - 2
                        s.wait_ge(s_cdel.h, j + 1)
                        s_csg.inc(s.activation(cw2[:, j:j + 1], cdel[:, j:j + 1],
                                               ACT.Sigmoid))
                s_bar.inc(s.nop())

            @blkD.tensor
            def _(t):
                t.wait_ge(s_bar.h, BAR_C)
                t.wait_ge(s_rwl.h, 16 * HCH)
                for i in range(LT):
                    t.wait_ge(s_cmid.h, i + 1)
                    for k in range(HCH):
                        if k >= 2:
                            t.wait_ge(s_cte.h, HCH * i + k - 1)
                        s_ctr.inc(t.transpose(
                            ps_ctr[:, 512 * (k % 2):512 * (k % 2) + 128],
                            mids[i % 2][:, 128 * k:128 * (k + 1)], iden_f[:, :]))
                    t.wait_ge(s_cte.h, HCH * (i + 1))
                    if i >= 1:
                        t.wait_ge(s_cw.h, i)
                    for k in range(HCH):
                        ins = t.matmul(ps_rt[:, 0:E],
                                       fmf32[:, 128 * k:128 * (k + 1)],
                                       rw_sb[:, E * k:E * (k + 1)],
                                       start=(k == 0), stop=(k == HCH - 1))
                    s_crt.inc(ins)
                s_bar.inc(t.nop())

            @blkD.vector
            def _(v):
                v = VG(v)
                v.wait_ge(s_bar.h, BAR_C)

                def ln1(i):
                    v.wait_ge(s_cadd.h, i + 1)
                    v.tensor_reduce(de1[:, :], sss[i % 2][:, :], AX.X, ALU.add)
                    v.tensor_scalar(cmu[:, i:i + 1], de1[:, :], 1.0 / H, None,
                                    ALU.mult)
                    v.wait_ge(s_csq.h, i + 1)
                    v.tensor_scalar(de2[:, :], csq[:, i:i + 1], 1.0 / H, None,
                                    ALU.mult)
                    v.tensor_mul(de3[:, :], cmu[:, i:i + 1], cmu[:, i:i + 1])
                    v.tensor_sub(de2[:, :], de2[:, :], de3[:, :])
                    s_cln.inc(v.tensor_scalar(ctln[:, i:i + 1], de2[:, :],
                                              EPS_LN, None, ALU.add))

                pertile_cmst = 32 + (16 if d_mid is not None else 0)

                def ln2(j):
                    v.wait_ge(s_csr.h, j + 1)
                    if j >= 2:
                        v.wait_ge(s_ctr2.h, 128 * (j - 1))
                        v.wait_ge(s_cmst.h, pertile_cmst * (j - 1))
                    v.reciprocal(cdtmp[:, :], csrt[:, j:j + 1])
                    v.tensor_scalar(cnmu[:, j:j + 1], cmu[:, j:j + 1], -1.0, None,
                                    ALU.mult)
                    v.tensor_scalar(mids[j % 2][:, :], sss[j % 2][:, :],
                                    cnmu[:, j:j + 1], cdtmp[:, :],
                                    ALU.add, ALU.mult)
                    s_cmid.inc(v.tensor_copy(midb[j % 2][:, :], mids[j % 2][:, :]))

                def trevac(j):
                    for k in range(HCH):
                        v.wait_ge(s_ctr.h, HCH * j + k + 1)
                        s_cte.inc(v.tensor_copy(
                            fmf32[:, 128 * k:128 * (k + 1)],
                            ps_ctr[:, 512 * (k % 2):512 * (k % 2) + 128]))

                def top2(j):
                    v.wait_ge(s_crt.h, j + 1)
                    v.tensor_copy(lrt[:, :], ps_rt[:, 0:E])
                    v.tensor_reduce(cmx[:, :], lrt[:, :], AX.X, ALU.max,
                                    apply_absolute_value=True)
                    v.tensor_scalar(ceps[:, :], cmx[:, :], 6e-8, None, ALU.mult)
                    v.scalar_tensor_tensor(lrp[:, :], negio8[:, :], ceps[:, :],
                                           lrt[:, :], ALU.mult, ALU.add)
                    v.tensor_reduce(dm1[:, :], lrp[:, :], AX.X, ALU.max)
                    v.tensor_scalar(oh1s[:, E * j:E * (j + 1)], lrp[:, :],
                                    dm1[:, :], None, ALU.is_ge)
                    v.scalar_tensor_tensor(lrpp[:, :], oh1s[:, E * j:E * (j + 1)],
                                           -BIG, lrp[:, :], ALU.mult, ALU.add)
                    v.tensor_reduce(dm2[:, :], lrpp[:, :], AX.X, ALU.max)
                    v.tensor_scalar(oh2s[:, E * j:E * (j + 1)], lrpp[:, :],
                                    dm2[:, :], None, ALU.is_ge)
                    s_cdel.inc(v.tensor_sub(cdel[:, j:j + 1], dm2[:, :], dm1[:, :]))

                def wtile(j):
                    v.wait_ge(s_csg.h, j + 1)
                    v.tensor_scalar(cw1[:, j:j + 1], cw2[:, j:j + 1], -1.0, 1.0,
                                    ALU.mult, ALU.add)
                    v.tensor_scalar(wffn_s[:, E * j:E * (j + 1)],
                                    oh1s[:, E * j:E * (j + 1)], cw1[:, j:j + 1],
                                    None, ALU.mult)
                    s_cw.inc(v.scalar_tensor_tensor(
                        wffn_s[:, E * j:E * (j + 1)], oh2s[:, E * j:E * (j + 1)],
                        cw2[:, j:j + 1], wffn_s[:, E * j:E * (j + 1)],
                        ALU.mult, ALU.add))

                for i in range(LT + 4):
                    if i < LT:
                        ln1(i)
                    if 1 <= i <= LT:
                        ln2(i - 1)
                    if 3 <= i < LT + 3:
                        wtile(i - 3)
                    if 2 <= i < LT + 2:
                        top2(i - 2)
                    if 1 <= i <= LT:
                        trevac(i - 1)
                s_bar.inc(v.nop())

        stD.close()
    BAR_D = BAR_C + 5

    if upto >= 5:
        stE = ExitStack()
        sbe, pse = mk_alloc(stE)
        wgu_sb = [sbe(f"wgu{i}", [128, HCH * GU_OUT], BF16) for i in range(2)]
        wdn_sb = [sbe(f"wdn{i}", [128, HCH * H], BF16) for i in range(2)]
        mfq = sbe("mfq", [128, TQ * H], BF16)
        accq = sbe("accq", [128, TQ * H])
        gub_p0 = [sbe("gub_p0_0", [1, GU_OUT])] * 2
        dnb_p0 = [sbe("dnb_p0_0", [1, H])] * 2
        wffn_pre = sbe("wffn_pre", [128, LT * E])
        gatec = sbe("gatec", [128, I_FFN])
        sgl = junkb[:, I_FFN:2 * I_FFN]
        actf = [sbe("actf0", [128, I_FFN], BF16)] * 2
        actb = [sbe(f"actb{i}", [128, I_FFN], BF16) for i in range(2)]
        fmu = sbe("fmu", [128, LT])
        fe1 = sbe("fe1", [128, 1])
        fe2 = sbe("fe2", [128, 1])
        fe3 = sbe("fe3", [128, 1])
        fnmu = sbe("fnmu", [128, LT])
        fsq = sbe("fsq", [128, LT])
        ftl = sbe("ftl", [128, LT])
        fsr = sbe("fsr", [128, LT])
        fetmp = sbe("fetmp", [128, 1])
        ps_gu = pse("ps_gu", [128, 2048])
        ps_dn = pse("ps_dn", [128, 1024])

        with nc.Block() as blkE:

            @blkE.gpsimd
            def _(g):
                g.wait_ge(s_bar.h, BAR_D)
                for k in range(QT * E):
                    e = k % E
                    if k >= 2:
                        g.wait_ge(s_fgu.h, 4 * TQ * (k - 1))
                        g.wait_ge(s_fdn.h, TQ * (k - 1))
                    for j in range(HCH):
                        s_fw.dinc(g.dma_start(
                            out=wgu_sb[k % 2][:, GU_OUT * j:GU_OUT * (j + 1)],
                            in_=wgu_d[e, bass.ts(j, 128), :]))
                    for j in range(HCH):
                        s_fw.dinc(g.dma_start(
                            out=wdn_sb[k % 2][:, H * j:H * (j + 1)],
                            in_=wdn_d[e, bass.ts(j, 128), :]))
                s_bar.inc(g.nop())

            @blkE.sync
            def _(sy):
                sy.wait_ge(s_bar.h, BAR_D)
                for q in range(QT):
                    if q >= 1:
                        sy.wait_ge(s_fgu.h, 4 * TQ * E * q)
                    for tl in range(TQ):
                        s_fmf.dinc(sy.dma_start(
                            out=mfq[:, H * tl:H * (tl + 1)]
                            .rearrange("p (k c2) -> p k c2", c2=128),
                            in_=midfm_d.ap()[bass.ts(q * TQ + tl, 1), :, :, :]
                            .rearrange("o k p c2 -> (o p) k c2")))
                    if q >= 1:
                        sy.wait_ge(s_fout.h, 16 * TQ * q)
                    for tl in range(TQ):
                        gtl = q * TQ + tl
                        s_fmid.dinc(sy.dma_start(
                            out=accq[:, H * tl:H * (tl + 1)],
                            in_=mid32_d.ap()[bass.ts(gtl, 128), :]))
                    for e in range(E):
                        k = q * E + e
                        if k >= 1:
                            sy.wait_ge(s_fgu.h, 4 * TQ * k)
                            sy.wait_ge(s_fdn.h, TQ * k)
                        s_fpre.dinc(sy.dma_start(out=gub_p0[k % 2][:, :],
                                                 in_=gub_d[0:1, GU_OUT * e:GU_OUT * (e + 1)]))
                        s_fpre.dinc(sy.dma_start(out=dnb_p0[k % 2][:, :],
                                                 in_=dnb_d[0:1, H * e:H * (e + 1)]))
                        for tl in range(TQ):
                            nt = k * TQ + tl
                            sy.wait_ge(s_fac.h, 4 * (nt + 1))
                            if nt >= 1:
                                sy.wait_ge(s_fdn.h, nt)
                            for kk in range(HCH):
                                s_ftr.dinc(sy.dma_start(
                                    out=actf[nt % 2][:, 128 * kk:128 * (kk + 1)],
                                    in_=actb[nt % 2][:, 128 * kk:128 * (kk + 1)],
                                    transpose=True))
                    for tl in range(TQ):
                        gtl = q * TQ + tl
                        sy.wait_ge(s_foutv.h, gtl + 1)
                        s_fout.dinc(sy.dma_start(
                            out=out_d[bass.ts(gtl, 128), :],
                            in_=accq[:, H * tl:H * (tl + 1)]))
                sy.wait_ge(s_fout.h, 16 * LT)
                s_bar.inc(sy.nop())

            @blkE.tensor
            def _(t):
                t.wait_ge(s_bar.h, BAR_D)
                for q in range(QT):
                    t.wait_ge(s_fmf.h, 16 * TQ * (q + 1))
                    for e in range(E):
                        k = q * E + e
                        t.wait_ge(s_fw.h, 16 * 16 * (k + 1))
                        t.wait_ge(s_fpre.h, 32 * (k + 1))
                        t.wait_ge(s_fpv.h, k + 1)
                        for tl in range(TQ + 1):
                            nt = k * TQ + tl
                            if tl < TQ:
                                for c in range(4):
                                    if nt >= 1:
                                        t.wait_ge(s_fac.h, 4 * (nt - 1) + c + 1)
                                    for j in range(HCH):
                                        t.matmul(
                                            ps_gu[:, 512 * c:512 * (c + 1)],
                                            mfq[:, H * tl + 128 * j:
                                                H * tl + 128 * (j + 1)],
                                            wgu_sb[k % 2][:, GU_OUT * j + 512 * c:
                                                          GU_OUT * j + 512 * (c + 1)],
                                            start=(j == 0), stop=False)
                                    s_fgu.inc(t.matmul(
                                        ps_gu[:, 512 * c:512 * (c + 1)],
                                        ones_r[:, :],
                                        gub_p0[k % 2][0:1, 512 * c:512 * (c + 1)],
                                        start=False, stop=True))
                            if tl >= 1:
                                ntp = k * TQ + tl - 1
                                t.wait_ge(s_ftr.h, 16 * HCH * (ntp + 1))
                                if ntp >= 1:
                                    t.wait_ge(s_facc.h, ntp)
                                for n in range(2):
                                    for j in range(HCH):
                                        t.matmul(
                                            ps_dn[:, 512 * n:512 * (n + 1)],
                                            actf[ntp % 2][:, 128 * j:128 * (j + 1)],
                                            wdn_sb[k % 2][:, H * j + 512 * n:
                                                          H * j + 512 * (n + 1)],
                                            start=(j == 0), stop=False)
                                    ins = t.matmul(
                                        ps_dn[:, 512 * n:512 * (n + 1)],
                                        ones_r[:, :],
                                        dnb_p0[k % 2][0:1, 512 * n:512 * (n + 1)],
                                        start=False, stop=True)
                                s_fdn.inc(ins)
                s_bar.inc(t.nop())

            @blkE.scalar
            def _(s):
                s.wait_ge(s_bar.h, BAR_D)
                for q in range(QT):
                    for e in range(E):
                        for tl in range(TQ):
                            nt = (q * E + e) * TQ + tl
                            for c in range(4):
                                s.wait_ge(s_fgc.h, 4 * nt + c + 1)
                                s_fsl.inc(s.activation(
                                    sgl[:, 256 * c:256 * (c + 1)],
                                    gatec[:, 256 * c:256 * (c + 1)], ACT.Silu,
                                    scale=ALPHA))
                    for tl in range(TQ):
                        gtl = q * TQ + tl
                        s.wait_ge(s_facc.h, q * E * TQ + (E - 1) * TQ + tl + 1)
                        s_fsq.inc(s.activation(junkb[:, 0:H],
                                               accq[:, H * tl:H * (tl + 1)],
                                               ACT.Square,
                                               accum_out=fsq[:, gtl:gtl + 1]))
                        s.wait_ge(s_fsr.h, gtl + 1)
                        s_fsq.inc(s.activation(fsr[:, gtl:gtl + 1],
                                               ftl[:, gtl:gtl + 1], ACT.Sqrt))
                s_bar.inc(s.nop())

            @blkE.vector
            def _(v):
                v = VG(v)
                v.wait_ge(s_bar.h, BAR_D)
                v.wait_ge(s_cw.h, LT)
                v.tensor_scalar(wffn_pre[:, :], wffn_s[:, :], 1.0 / ALPHA, None,
                                ALU.mult)

                for q in range(QT):
                    for e in range(E):
                        k = q * E + e
                        v.wait_ge(s_fpre.h, 32 * (k + 1))
                        s_fpv.inc(v.tensor_scalar(dnb_p0[k % 2][:, :],
                                                  dnb_p0[k % 2][:, :],
                                                  ALPHA, None, ALU.mult))
                        for tl in range(TQ):
                            nt = (q * E + e) * TQ + tl
                            for c in range(4):
                                v.wait_ge(s_fgu.h, 4 * nt + c + 1)
                                s_fgc.inc(v.tensor_scalar(
                                    gatec[:, 256 * c:256 * (c + 1)],
                                    ps_gu[:, 512 * c:512 * (c + 1)][:, 0::2],
                                    LIMIT, None, ALU.min))
                                v.wait_ge(s_fsl.h, 4 * nt + c + 1)
                                if nt >= 2 and c == 0:
                                    v.wait_ge(s_ftr.h, 16 * HCH * (nt - 1))
                                v.tensor_scalar(
                                    gatec[:, 256 * c:256 * (c + 1)],
                                    ps_gu[:, 512 * c:512 * (c + 1)][:, 1::2],
                                    LIMIT, -LIMIT, ALU.min, ALU.max)
                                v.tensor_scalar(gatec[:, 256 * c:256 * (c + 1)],
                                                gatec[:, 256 * c:256 * (c + 1)],
                                                1.0, None, ALU.add)
                                s_fac.inc(v.tensor_mul(
                                    actb[nt % 2][:, 256 * c:256 * (c + 1)],
                                    gatec[:, 256 * c:256 * (c + 1)],
                                    sgl[:, 256 * c:256 * (c + 1)]))
                            v.wait_ge(s_fdn.h, nt + 1)
                            gtl = q * TQ + tl
                            if e == 0:
                                v.wait_ge(s_fmid.h, 16 * (gtl + 1))
                            s_facc.inc(v.scalar_tensor_tensor(
                                accq[:, H * tl:H * (tl + 1)], ps_dn[:, 0:H],
                                wffn_pre[:, E * gtl + e:E * gtl + e + 1],
                                accq[:, H * tl:H * (tl + 1)], ALU.mult, ALU.add))
                    for tl in range(TQ):
                        gtl = q * TQ + tl
                        v.tensor_reduce(fe1[:, :], accq[:, H * tl:H * (tl + 1)],
                                        AX.X, ALU.add)
                        v.tensor_scalar(fmu[:, gtl:gtl + 1], fe1[:, :], 1.0 / H,
                                        None, ALU.mult)
                        v.wait_ge(s_fsq.h, 2 * gtl + 1)
                        v.tensor_scalar(fe2[:, :], fsq[:, gtl:gtl + 1], 1.0 / H,
                                        None, ALU.mult)
                        v.tensor_mul(fe3[:, :], fmu[:, gtl:gtl + 1],
                                     fmu[:, gtl:gtl + 1])
                        v.tensor_sub(fe2[:, :], fe2[:, :], fe3[:, :])
                        s_fsr.inc(v.tensor_scalar(ftl[:, gtl:gtl + 1], fe2[:, :],
                                                  EPS_LN, None, ALU.add))
                        v.wait_ge(s_fsq.h, 2 * gtl + 2)
                        v.reciprocal(fetmp[:, :], fsr[:, gtl:gtl + 1])
                        v.tensor_scalar(fnmu[:, gtl:gtl + 1], fmu[:, gtl:gtl + 1],
                                        -1.0, None, ALU.mult)
                        s_foutv.inc(v.tensor_scalar(
                            accq[:, H * tl:H * (tl + 1)],
                            accq[:, H * tl:H * (tl + 1)],
                            fnmu[:, gtl:gtl + 1], fetmp[:, :],
                            ALU.add, ALU.mult))
                s_bar.inc(v.nop())

        stE.close()

    st0.close()
    return nc, dbg_out


def make_in_maps(inputs, names=None):
    x = np.ascontiguousarray(np.asarray(inputs["L0_attn_x"],
                                        np.float32).reshape(T, H))
    gnw = np.asarray(inputs["L0_attn_gate_norm_w"], np.float32).reshape(1, H)
    ginw = np.asarray(inputs["L0_attn_gate_in_norm_w"], np.float32).reshape(1, H)
    gatew = np.ascontiguousarray(np.asarray(inputs["L0_attn_gate_w"], np.float32))
    gpn = np.ascontiguousarray(np.asarray(inputs["L0_attn_gp_norm_w"], np.float32))
    gpw = np.ascontiguousarray(np.asarray(inputs["L0_attn_gp_w"], np.float32))
    dpn = np.ascontiguousarray(np.asarray(inputs["L0_attn_dp_norm_w"], np.float32))
    dpw = np.ascontiguousarray(np.asarray(inputs["L0_attn_dp_w"], np.float32))
    rw = np.ascontiguousarray(np.asarray(inputs["L0_ffn_router_weight"], np.float32))
    wgu = np.ascontiguousarray(np.asarray(inputs["L0_ffn_gate_up_proj"], np.float32))
    gub = np.ascontiguousarray(np.asarray(inputs["L0_ffn_gate_up_proj_bias"],
                                          np.float32).reshape(1, E * GU_OUT))
    wdn = np.ascontiguousarray(np.asarray(inputs["L0_ffn_down_proj"], np.float32))
    dnb = np.ascontiguousarray(np.asarray(inputs["L0_ffn_down_proj_bias"],
                                          np.float32).reshape(1, E * H))
    maps = []
    for c in range(NCORE):
        maps.append({
            "x": x, "xsh": np.ascontiguousarray(x[c * TPC:(c + 1) * TPC]),
            "gnw": gnw, "ginw": ginw, "gatew": gatew,
            "gpn": np.ascontiguousarray(gpn[c:c + 1, :]),
            "gpw": np.ascontiguousarray(gpw[c]),
            "dpn": np.ascontiguousarray(dpn[c:c + 1, :]),
            "dpw": np.ascontiguousarray(dpw[c]),
            "rw": rw, "wgu": wgu, "gub": gub, "wdn": wdn, "dnb": dnb,
        })
    if names is not None:
        maps = [{k: v for k, v in m.items() if k in names} for m in maps]
    return maps


def kernel(**inputs):
    nc, _ = build_nc()
    maps = make_in_maps(inputs)
    res = run_bass_kernel_spmd(nc, maps, core_ids=list(range(NCORE)))
    out = np.concatenate([res.results[c]["out"] for c in range(NCORE)], axis=0)
    return out.reshape(NCORE, TPC, H)


if __name__ == "__main__":
    nc, dbg = build_nc(upto=2, debug=("logits", "mask", "idw", "pos", "cidx"))
    print("built upto=2, instructions:",
          sum(1 for _ in nc.all_instructions()))
